# version 01
# baseline (speedup 1.0000x reference)
import numpy as np
import ml_dtypes
from contextlib import ExitStack

import concourse.bass as bass
import concourse.mybir as mybir
from concourse.bass_utils import run_bass_kernel_spmd

F32 = mybir.dt.float32
BF16 = mybir.dt.bfloat16
AF = mybir.ActivationFunctionType
ALU = mybir.AluOpType
AX = mybir.AxisListType

D = 2048
T = 2048
H = 16
E = 128
C = 64
FF = 8192
NI = D // 128
NT = T // 128
SUP = 512
NSUP = T // SUP
NO = FF // 128

MAGIC = float(np.float32(1.5 * 2 ** 23))
EPS_RMS = 1e-6
EPS_LN = 1e-5
EPS_Q = 1e-5

ENGINES = ("sync", "gpsimd", "vector", "scalar", "tensor")


class Prog:
    def __init__(self):
        self.ops = {e: [] for e in ENGINES}
        self.cnt = {}
        self.hw = {}
        self.sems = {}

    def add_sem(self, name, handle):
        self.sems[name] = handle
        self.cnt[name] = 0

    def op(self, eng, fn):
        self.ops[eng].append(fn)

    def inc(self, sem, n):
        self.cnt[sem] += n
        return self.cnt[sem]

    def wait(self, eng, sem, val=None):
        v = self.cnt[sem] if val is None else val
        if v <= 0:
            return
        key = (eng, sem)
        if v <= self.hw.get(key, 0):
            return
        self.hw[key] = v
        h = self.sems[sem]
        self.op(eng, lambda e, h=h, v=v: e.wait_ge(h, v))

    def dma(self, eng, out, in_, sem, transpose=False, slow=False):
        self.inc(sem, 16)
        h = self.sems[sem]

        def fn(e, out=out, in_=in_, t=transpose, h=h, slow=slow):
            kw = dict(allow_slow_non_contiguous=True) if slow else {}
            e.dma_start(out=out, in_=in_, transpose=t, **kw).then_inc(h, 16)
        self.op(eng, fn)
        return self.cnt[sem]

    def run(self, eng, fn, sem=None, n=1):
        if sem is None:
            self.op(eng, fn)
            return None
        self.inc(sem, n)
        h = self.sems[sem]

        def wrapped(e, fn=fn, h=h, n=n):
            fn(e).then_inc(h, n)
        self.op(eng, wrapped)
        return self.cnt[sem]


class Buf:

    def __init__(self, prog, writer):
        self.P = prog
        self.writer = writer
        self.pending = {}

    def acquire(self):
        for s, v in self.pending.items():
            self.P.wait(self.writer, s, v)
        self.pending.clear()

    def consumed(self, sem, val=None):
        v = self.P.cnt[sem] if val is None else val
        self.pending[sem] = max(self.pending.get(sem, 0), v)


class Arena:
    def __init__(self, nwords):
        self.N = nwords
        self.free = [(0, nwords)]
        self.named = {}
        self.peak = 0

    def alloc(self, name, words):
        words = (words + 15) & ~15
        for i, (off, ln) in enumerate(self.free):
            if ln >= words:
                self.named[name] = (off, words)
                if ln == words:
                    self.free.pop(i)
                else:
                    self.free[i] = (off + words, ln - words)
                used = self.N - sum(l for _, l in self.free)
                self.peak = max(self.peak, used)
                return off
        raise MemoryError(f"SBUF arena OOM for {name} ({words} w); free={self.free}")

    def freeb(self, *names):
        for name in names:
            off, words = self.named.pop(name)
            self.free.append((off, words))
        self.free.sort()
        out = []
        for o, l in self.free:
            if out and out[-1][0] + out[-1][1] == o:
                out[-1] = (out[-1][0], out[-1][1] + l)
            else:
                out.append([o, l])
        self.free = [tuple(x) for x in out]


def build_nc(dump=(), upto=99):
    nc = bass.Bass("TRN2", target_bir_lowering=False)

    def dram(name, shape, dtype, kind):
        if name in dump and kind == "Internal":
            kind = "ExternalOutput"
        if name.startswith("dbg"):
            kind = "ExternalOutput"
        return nc.dram_tensor(name, shape, dtype, kind=kind).ap()

    x_in = dram("x", [T, D], F32, "ExternalInput")
    Wq = dram("Wq", [D, D], F32, "ExternalInput")
    Wk = dram("Wk", [D, D], F32, "ExternalInput")
    Wv = dram("Wv", [D, D], F32, "ExternalInput")
    Wf = dram("Wf", [D, H], F32, "ExternalInput")
    Wo = dram("Wo", [D, D], F32, "ExternalInput")
    nw_in = dram("norm_w", [D], F32, "ExternalInput")
    wg_in = dram("w_gate", [FF, D], F32, "ExternalInput")
    wu_in = dram("w_up", [FF, D], F32, "ExternalInput")
    wd_in = dram("w_down", [D, FF], F32, "ExternalInput")

    maskt_in = dram("maskt", [64, 64], F32, "ExternalInput")
    sela_in = dram("sela", [128, 128], F32, "ExternalInput")
    selb_in = dram("selb", [128, 128], F32, "ExternalInput")
    onesf_in = dram("onesf", [128, 128], F32, "ExternalInput")
    identf_in = dram("identf", [128, 128], F32, "ExternalInput")
    consts_in = dram("consts", [128, 8], F32, "ExternalInput")

    out_d = dram("out", [T, D], F32, "ExternalOutput")

    sxb = dram("sxb", [T, D], BF16, "Internal")
    sxr = dram("sxr", [T, D], BF16, "Internal")
    sq = dram("sq", [T, D], BF16, "Internal")
    sk = dram("sk", [T, D], BF16, "Internal")
    sv = dram("sv", [T, D], BF16, "Internal")
    sx1 = dram("sx1", [T, D], F32, "Internal")
    sxq = dram("sxq", [T, D], BF16, "Internal")
    sh = dram("sh", [T, FF], F32, "Internal")
    shq = dram("shq", [T, FF], BF16, "Internal")
    swg = dram("swg", [FF, D], BF16, "Internal")
    swu = dram("swu", [FF, D], BF16, "Internal")
    swd = dram("swd", [D, FF], BF16, "Internal")

    P = Prog()
    ARF = Arena(12000)
    ARB = Arena(70000)

    with ExitStack() as stack:
        arena_t = stack.enter_context(nc.sbuf_tensor("arena", [128, ARF.N], F32))
        arena = arena_t[:]
        arenab_t = stack.enter_context(nc.sbuf_tensor("arenab", [128, ARB.N], BF16))
        arenab = arenab_t[:]

        def f32v(name, parts, words):
            off = ARF.alloc(name, words)
            return arena[0:parts, off:off + words]

        def bf16v(name, parts, nelem):
            off = ARB.alloc(name, nelem)
            return arenab[0:parts, off:off + nelem]

        class _AR:
            @staticmethod
            def freeb(*names):
                for n in names:
                    if n in ARF.named:
                        ARF.freeb(n)
                    else:
                        ARB.freeb(n)
        AR = _AR()

        psum = [stack.enter_context(nc.psum_tensor(f"pb{i}", [128, 512], F32))[:]
                for i in range(8)]

        for s in ("dS", "dG", "dA", "t", "v", "a"):
            P.add_sem(s, stack.enter_context(nc.semaphore(s)))

        banks = [Buf(P, "tensor") for _ in range(8)]

        def full_barrier():
            banks[7].acquire()
            P.run("tensor", lambda e: e.matmul(
                psum[7][0:1, 0:1], ONESF[0:1, 0:1], ONESF[0:1, 0:1],
                start=True, stop=True), "t")
            P.run("vector", lambda e: e.tensor_copy(
                out=SMALL[:, 15:16], in_=SMALL[:, 14:15]), "v")
            P.run("scalar", lambda e: e.copy(
                out=SMALL[:, 13:14], in_=SMALL[:, 12:13]), "a")
            for eng in ENGINES:
                for sm in ("t", "v", "a", "dS", "dG", "dA"):
                    P.wait(eng, sm)

        CONSTS = f32v("consts", 128, 8)
        NW = f32v("nw", 128, 16)
        MASKT = f32v("maskt", 64, 64)
        SELA = f32v("sela", 128, 128)
        SELB = f32v("selb", 128, 128)
        ONESF = f32v("onesf", 128, 128)
        IDENTF = f32v("identf", 128, 128)
        MAGICC = CONSTS[:, 0:1]
        EPSLNC = CONSTS[:, 1:2]
        EPSRMSC = CONSTS[:, 2:3]

        HMAXTM = f32v("hmaxtm", 128, NT)
        S1COL = f32v("s1col", 128, NT)
        S1WG = f32v("s1wg", 128, NT)
        S1WU = f32v("s1wu", 128, NT)
        S2WD = f32v("s2wd", 128, NT)
        RS2TM = f32v("rs2tm", 128, NT)
        WSCOL = f32v("wscol", 128, 4)
        HWCOL = f32v("hwcol", 128, 4)
        NHWCOL = f32v("nhwcol", 128, 4)
        SMALL = f32v("small", 128, 16)
        PREFTM = f32v("preftm", 128, 16 * NT)
        KSCTM = f32v("ksctm", 128, 16 * NT)
        ETOTTM = f32v("etottm", 128, 32 * NT)
        RPRODTM = f32v("rprodtm", 128, 32 * NT)
        SF32 = f32v("sf32", 128, 128 * H)
        SBF = bf16v("sbf", 128, 128 * H)

        P.dma("sync", CONSTS, consts_in, "dS")
        P.dma("sync", NW, nw_in.rearrange("(c p) -> p c", p=128), "dS", slow=True)
        P.dma("sync", MASKT, maskt_in, "dS")
        P.dma("sync", SELA, sela_in, "dS")
        P.dma("sync", SELB, selb_in, "dS")
        P.dma("sync", ONESF, onesf_in, "dS")
        P.dma("sync", IDENTF, identf_in, "dS")
        for e in ("vector", "scalar", "tensor", "gpsimd"):
            P.wait(e, "dS")
        P.run("vector", lambda e: e.memset(SF32, 0.0))
        P.run("vector", lambda e: e.memset(SBF, 0.0))
        P.run("vector", lambda e: e.memset(HMAXTM, 0.0), "v")

        if upto >= 1:
            WSL = [f32v("wslab0", 128, 2048), f32v("wslab1", 128, 2048)]
            wslab_buf = [Buf(P, "sync") for _ in range(2)]
            BSL = f32v("bslab", 128, 2048)
            QSL = bf16v("qslab", 128, 2048)
            PACC = f32v("pacc", 128, 4)
            PARTS = f32v("parts", 128, 64)
            nelems = [FF * D, FF * D, D * FF]

            def slab_aps(wi):
                win = (wg_in, wu_in, wd_in)[wi]
                wout = (swg, swu, swd)[wi]
                res = []
                if wi < 2:
                    for r in range(FF // 128):
                        rs = slice(r * 128, (r + 1) * 128)
                        res.append((win[rs, :], wout[rs, :]))
                else:
                    for r in range(D // 128):
                        rs = slice(r * 128, (r + 1) * 128)
                        for cb in range(FF // 2048):
                            cs = slice(cb * 2048, (cb + 1) * 2048)
                            res.append((win[rs, cs], wout[rs, cs]))
                return res

            for wi in range(3):
                for si, (rap, _) in enumerate(slab_aps(wi)):
                    b = si % 2
                    wslab_buf[b].acquire()
                    P.dma("sync", WSL[b], rap, "dS")
                    P.wait("vector", "dS")
                    P.run("vector", lambda e, b=b, si=si: e.reduce_sum(
                        out=PARTS[:, si:si + 1], in_=WSL[b], axis=AX.X,
                        apply_absolute_value=True), "v")
                    wslab_buf[b].consumed("v")
                P.wait("vector", "v")
                P.run("vector", lambda e, wi=wi: e.reduce_sum(
                    out=PACC[:, wi:wi + 1], in_=PARTS, axis=AX.X), "v")
                banks[4].acquire()
                P.wait("tensor", "v")
                P.run("tensor", lambda e, wi=wi: e.matmul(
                    psum[4][:, 0:1], ONESF, PACC[:, wi:wi + 1],
                    start=True, stop=True), "t")
                P.wait("vector", "t")
                P.run("vector", lambda e, wi=wi, n=1.0 / nelems[wi]: e.tensor_scalar(
                    out=WSCOL[:, wi:wi + 1], in0=psum[4][:, 0:1], scalar1=n,
                    scalar2=EPS_Q, op0=ALU.mult, op1=ALU.add), "v")
                banks[4].consumed("v")
                P.wait("vector", "v")
                P.run("vector", lambda e, wi=wi: e.tensor_scalar(
                    out=HWCOL[:, wi:wi + 1], in0=WSCOL[:, wi:wi + 1], scalar1=0.5,
                    scalar2=None, op0=ALU.mult))
                P.run("vector", lambda e, wi=wi: e.tensor_scalar(
                    out=NHWCOL[:, wi:wi + 1], in0=WSCOL[:, wi:wi + 1], scalar1=-0.5,
                    scalar2=None, op0=ALU.mult), "v")
            P.wait("vector", "v")

            qsl_buf = Buf(P, "vector")
            for wi in range(3):
                for si, (rap, wap) in enumerate(slab_aps(wi)):
                    b = si % 2
                    wslab_buf[b].acquire()
                    P.dma("sync", WSL[b], rap, "dS")
                    P.wait("vector", "dS")
                    P.run("vector", lambda e, b=b, wi=wi: e.tensor_scalar(
                        out=BSL, in0=WSL[b], scalar1=NHWCOL[:, wi:wi + 1],
                        scalar2=None, op0=ALU.is_le))
                    qsl_buf.acquire()
                    P.run("vector", lambda e, b=b, wi=wi: e.scalar_tensor_tensor(
                        out=QSL, in0=WSL[b], scalar=HWCOL[:, wi:wi + 1],
                        in1=BSL, op0=ALU.is_ge, op1=ALU.subtract), "v")
                    wslab_buf[b].consumed("v")
                    P.wait("gpsimd", "v")
                    P.dma("gpsimd", wap, QSL, "dG")
                    qsl_buf.consumed("dG")
            AR.freeb("wslab0", "wslab1", "bslab", "qslab", "pacc", "parts")

        if upto >= 2:
            full_barrier()
            XT = f32v("ph1_xt", 128, 2048)
            XBv = bf16v("ph1_xb", 128, 2048)
            XRv = bf16v("ph1_xr", 128, 2048)
            xt_buf = Buf(P, "sync")
            xb_buf = Buf(P, "gpsimd")
            xr_buf = Buf(P, "vector")
            for tt in range(NT):
                rows = slice(tt * 128, (tt + 1) * 128)
                xt_buf.acquire()
                P.dma("sync", XT, x_in[rows, :], "dS")
                xb_buf.acquire()
                P.dma("gpsimd", XBv, x_in[rows, :], "dG")
                P.wait("vector", "dS")
                P.wait("vector", "dG")
                xr_buf.acquire()
                P.run("vector", lambda e: e.tensor_sub(out=XRv, in0=XT, in1=XBv), "v")
                xt_buf.consumed("v")
                P.wait("gpsimd", "v")
                P.dma("gpsimd", sxb[rows, :], XBv, "dG")
                xb_buf.consumed("dG")
                P.wait("gpsimd", "v")
                P.dma("gpsimd", sxr[rows, :], XRv, "dG")
                xr_buf.consumed("dG")
            AR.freeb("ph1_xt", "ph1_xb", "ph1_xr")

            XBT = bf16v("xbt", 128, NI * 2048)

            def xbt_c(c):
                return XBT[:, c * 2048:(c + 1) * 2048]
            P.wait("sync", "dG")
            for c in range(NI):
                P.dma("sync", xbt_c(c), sxb[0:T, c * 128:(c + 1) * 128], "dS",
                      transpose=True)

        if upto >= 3:
            full_barrier()
            WFB = bf16v("wfb", 128, H * NI)
            WFR = bf16v("wfr", 128, H * NI)
            WFT = f32v("wft", 128, H * NI)
            XRT = bf16v("xrt", 128, NI * 2048)

            def xrt_c(c):
                return XRT[:, c * 2048:(c + 1) * 2048]
            P.wait("sync", "dG")
            for c in range(NI):
                P.dma("sync", xrt_c(c), sxr[0:T, c * 128:(c + 1) * 128], "dS",
                      transpose=True)
                P.dma("sync", WFT[:, c * 16:(c + 1) * 16],
                      Wf[c * 128:(c + 1) * 128, :], "dS")
            P.wait("vector", "dS")
            P.run("vector", lambda e: e.tensor_copy(out=WFB, in_=WFT), "v")
            P.wait("vector", "v")
            P.run("vector", lambda e: e.tensor_sub(
                out=WFR, in0=WFT, in1=WFB), "v")
            if "dbg2f" in dump:
                dbgw_d = dram("dbgwf", [128, 768], F32, "Internal")
                DBGW2 = f32v("dbgw2", 128, 512)
                P.wait("vector", "v")
                for c in range(NI):
                    cs = slice(c * 16, (c + 1) * 16)
                    P.run("vector", lambda e, c=c, cs=cs: e.tensor_copy(
                        out=DBGW2[:, c * 16:(c + 1) * 16], in_=WFB[:, cs]))
                    P.run("vector", lambda e, c=c, cs=cs: e.tensor_copy(
                        out=DBGW2[:, 256 + c * 16:256 + (c + 1) * 16], in_=WFR[:, cs]), "v")
                P.wait("gpsimd", "v")
                P.dma("gpsimd", dbgw_d[0:128, 0:256], WFT, "dG")
                P.dma("gpsimd", dbgw_d[0:128, 256:768], DBGW2, "dG")
                AR.freeb("dbgw2")
            AR.freeb("wft")

            SIGT = f32v("sigt", 16, 2048)
            PREFT = f32v("preft", 16, 2048)
            SIGTM = f32v("sigtm", 128, 16)
            sigtm_buf = Buf(P, "scalar")
            sigt_buf = Buf(P, "vector")
            P.run("scalar", lambda e: e.preload_act_table(AF.Sigmoid))
            P.wait("tensor", "v")
            for tt in range(NT):
                tcols = slice(tt * 128, (tt + 1) * 128)
                bank = banks[tt % 2]
                bank.acquire()
                pf = psum[tt % 2][:, 0:16]
                for rnd in range(3):
                    for c in range(NI):
                        cs = slice(c * 16, (c + 1) * 16)
                        if rnd == 0:
                            lhs, rhs = xbt_c(c)[:, tcols], WFB[:, cs]
                        elif rnd == 1:
                            lhs, rhs = xrt_c(c)[:, tcols], WFB[:, cs]
                        else:
                            lhs, rhs = xbt_c(c)[:, tcols], WFR[:, cs]
                        last = (rnd == 2 and c == NI - 1)
                        P.run("tensor", lambda e, lhs=lhs, rhs=rhs, pf=pf,
                              s=(rnd == 0 and c == 0), st=last: e.matmul(
                            pf, lhs, rhs, start=s, stop=st),
                            "t" if last else None)
                P.wait("scalar", "t")
                sigtm_buf.acquire()
                P.run("scalar", lambda e, pf=pf: e.activation(
                    out=SIGTM, in_=pf, func=AF.Sigmoid), "a")
                bank.consumed("a")
                bank2 = banks[2 + tt % 2]
                bank2.acquire()
                pt = psum[2 + tt % 2][0:16, 0:128]
                P.wait("tensor", "a")
                P.run("tensor", lambda e, pt=pt: e.transpose(pt, SIGTM, IDENTF), "t")
                sigtm_buf.consumed("t")
                P.wait("vector", "t")
                sigt_buf.acquire()
                P.run("vector", lambda e, pt=pt, tcols=tcols: e.tensor_copy(
                    out=SIGT[:, tcols], in_=pt), "v")
                bank2.consumed("v")
            P.wait("vector", "v")
            for ch in range(T // C):
                cc = slice(ch * C, (ch + 1) * C)
                P.run("vector", lambda e, cc=cc: e.tensor_tensor_scan(
                    out=PREFT[:, cc], data0=SIGT[:, cc], data1=SIGT[:, cc],
                    initial=1.0, op0=ALU.mult, op1=ALU.bypass),
                    "v" if ch == T // C - 1 else None)
            P.wait("tensor", "v")
            for tt in range(NT):
                tcols = slice(tt * 128, (tt + 1) * 128)
                kcols = slice(tt * 16, (tt + 1) * 16)
                ccols = slice(tt * 32, (tt + 1) * 32)
                bank = banks[4 + tt % 2]
                bank.acquire()
                pt = psum[4 + tt % 2][:, 0:16]
                P.run("tensor", lambda e, pt=pt, tcols=tcols: e.transpose(
                    pt, PREFT[:, tcols], IDENTF[0:16, 0:16]), "t")
                P.wait("vector", "t")
                P.run("vector", lambda e, pt=pt, kcols=kcols: e.tensor_copy(
                    out=PREFTM[:, kcols], in_=pt), "v")
                bank.consumed("v")
                bank2 = banks[6 + tt % 2]
                bank2.acquire()
                pe_ = psum[6 + tt % 2][:, 0:32]
                P.wait("tensor", "v")
                P.run("tensor", lambda e, pe_=pe_, kcols=kcols: e.matmul(
                    pe_[:, 0:16], SELA, PREFTM[:, kcols], start=True, stop=True))
                P.run("tensor", lambda e, pe_=pe_, kcols=kcols: e.matmul(
                    pe_[:, 16:32], SELB, PREFTM[:, kcols], start=True, stop=True), "t")
                P.wait("vector", "t")
                P.run("vector", lambda e, pe_=pe_, ccols=ccols: e.tensor_copy(
                    out=ETOTTM[:, ccols], in_=pe_))
                P.run("vector", lambda e, pe_=pe_, ccols=ccols: e.tensor_scalar(
                    out=RPRODTM[:, ccols], in0=pe_, scalar1=1e-37, scalar2=None,
                    op0=ALU.max), "v")
                bank2.consumed("v")
                P.wait("vector", "v")
                P.run("vector", lambda e, ccols=ccols: e.reciprocal(
                    out=RPRODTM[:, ccols], in_=RPRODTM[:, ccols]), "v")
            if "dbgt" in dump:
                dbgA = dram("dbgtA", [128, 256], F32, "Internal")
                P.wait("gpsimd", "v")
                P.dma("gpsimd", dbgA, PREFTM, "dG")
            RPREFTM = f32v("rpreftm", 128, 16 * NT)
            P.wait("vector", "v")
            P.run("vector", lambda e: e.reciprocal(out=RPREFTM, in_=PREFTM), "v")
            P.wait("vector", "v")
            for tt in range(NT):
                kcols = slice(tt * 16, (tt + 1) * 16)
                P.run("vector", lambda e, tt=tt, kcols=kcols: e.tensor_copy(
                    out=KSCTM[0:64, kcols],
                    in_=ETOTTM[0:64, tt * 32:tt * 32 + 16]))
                P.run("vector", lambda e, tt=tt, kcols=kcols: e.tensor_copy(
                    out=KSCTM[64:128, kcols],
                    in_=ETOTTM[64:128, tt * 32 + 16:tt * 32 + 32]), "v")
            P.wait("vector", "v")
            for tt in range(NT):
                kcols = slice(tt * 16, (tt + 1) * 16)
                P.run("vector", lambda e, kcols=kcols: e.tensor_mul(
                    out=KSCTM[:, kcols], in0=KSCTM[:, kcols],
                    in1=RPREFTM[:, kcols]), "v")
            if "dbgx" in dump:
                import os as _os
                _doff = int(_os.environ.get("DBGX_OFF", "8000"))
                _dsrc = _os.environ.get("DBGX_SRC", "Wq")
                _src = {"Wq": Wq, "Wk": Wk, "wg": wg_in}[_dsrc]
                XST = f32v("dbgx_stage", 128, 2048)
                full_barrier()
                P.dma("sync", arena[0:128, _doff:_doff + 2048] if _doff != -1 else XST,
                      _src[0:128, 0:2048], "dS")
                P.wait("sync", "dS")
                full_barrier()
            if "dbg2f" in dump:
                dbg_d = dram("dbg2f", [128, 4096], F32, "Internal")
                DBGW = f32v("dbgw", 128, 512)
                P.wait("vector", "a")
                P.wait("vector", "v")
                for c in range(NI):
                    P.run("vector", lambda e, c=c: e.tensor_copy(
                        out=DBGW[:, c * 16:(c + 1) * 16], in_=WFB[:, c * 16:(c + 1) * 16]))
                    P.run("vector", lambda e, c=c: e.tensor_copy(
                        out=DBGW[:, 256 + c * 16:256 + (c + 1) * 16],
                        in_=WFR[:, c * 16:(c + 1) * 16]), "v")
                P.wait("gpsimd", "v")
                P.dma("gpsimd", dbg_d[0:128, 3584 + 512:3584 + 512 - 512 + 512] if False else dbg_d[0:128, 3072 + 512:4096], DBGW, "dG")
                P.wait("gpsimd", "v")
                P.dma("gpsimd", dbg_d[0:128, 0:256], PREFTM, "dG")
                P.dma("gpsimd", dbg_d[0:128, 256:512], KSCTM, "dG")
                P.dma("gpsimd", dbg_d[0:128, 512:1024], ETOTTM, "dG")
                P.dma("gpsimd", dbg_d[0:128, 1024:1536], RPRODTM, "dG")
                P.dma("gpsimd", dbg_d[0:16, 1536:3584], SIGT, "dG")
                P.dma("gpsimd", dbg_d[0:16, 3584 + 0:3584 + 512], PREFT[:, 0:512], "dG")
            if "dbgt" in dump:
                dbgB = dram("dbgtB", [128, 256], F32, "Internal")
                P.wait("gpsimd", "v")
                P.dma("gpsimd", dbgB, PREFTM, "dG")
            AR.freeb("rpreftm", "sigt", "preft", "sigtm", "xrt", "wfb", "wfr")
            P.wait("vector", "v")

        if upto >= 4:
            full_barrier()
            WB = bf16v("wb", 128, NI * 2048)

            def wb_c(c):
                return WB[:, c * 2048:(c + 1) * 2048]
            QS = [f32v("qs0", 128, 512), f32v("qs1", 128, 512)]
            qs_buf = [Buf(P, "scalar") for _ in range(2)]
            OTILE = bf16v("otile", 128, 2048)
            wb_buf = Buf(P, "scalar")
            P.run("scalar", lambda e: e.preload_act_table(AF.Silu))

            WSTG = [f32v("wstg0", 128, 2048), f32v("wstg1", 128, 2048)]
            wstg_buf = [Buf(P, "sync") for _ in range(2)]
            for wsrc, sdst, do_silu, scale_tm in (
                    (Wq, sq, True, PREFTM), (Wk, sk, True, KSCTM), (Wv, sv, False, None)):
                wb_buf.acquire()
                for c in range(NI):
                    sb_ = wstg_buf[c % 2]
                    sb_.acquire()
                    P.dma("sync", WSTG[c % 2], wsrc[c * 128:(c + 1) * 128, :], "dS")
                    P.wait("scalar", "dS")
                    P.run("scalar", lambda e, c=c: e.copy(
                        out=wb_c(c), in_=WSTG[c % 2]), "a")
                    sb_.consumed("a")
                P.wait("tensor", "a")
                for tt in range(NT):
                    tcols = slice(tt * 128, (tt + 1) * 128)
                    for sp in range(4):
                        gi = tt * 4 + sp
                        bank = banks[gi % 4]
                        bank.acquire()
                        pw = psum[gi % 4][:, 0:512]
                        for c in range(NI):
                            P.run("tensor", lambda e, c=c, pw=pw, tcols=tcols, sp=sp,
                                  s=(c == 0), st=(c == NI - 1): e.matmul(
                                pw, xbt_c(c)[:, tcols],
                                wb_c(c)[:, sp * 512:(sp + 1) * 512],
                                start=s, stop=st), "t" if c == NI - 1 else None)
                        wb_buf.consumed("t")
                        if do_silu:
                            qb = qs_buf[gi % 2]
                            qb.acquire()
                            P.wait("scalar", "t")
                            P.run("scalar", lambda e, pw=pw, q=QS[gi % 2]: e.activation(
                                out=q, in_=pw, func=AF.Silu), "a")
                            bank.consumed("a")
                            P.wait("vector", "a")
                            for blk in range(4):
                                hcol = sp * 4 + blk
                                P.run("vector", lambda e, q=QS[gi % 2], blk=blk, sp=sp,
                                      tt=tt, hcol=hcol, stm=scale_tm:
                                      e.scalar_tensor_tensor(
                                    out=OTILE[:, sp * 512 + blk * 128:
                                              sp * 512 + (blk + 1) * 128],
                                    in0=q[:, blk * 128:(blk + 1) * 128],
                                    scalar=stm[:, tt * 16 + hcol:tt * 16 + hcol + 1],
                                    in1=q[:, blk * 128:(blk + 1) * 128],
                                    op0=ALU.mult, op1=ALU.bypass),
                                    "v" if blk == 3 else None)
                            qb.consumed("v")
                        else:
                            P.wait("vector", "t")
                            P.run("vector", lambda e, pw=pw, sp=sp: e.tensor_copy(
                                out=OTILE[:, sp * 512:(sp + 1) * 512], in_=pw), "v")
                            bank.consumed("v")
                    P.wait("gpsimd", "v")
                    P.dma("gpsimd", sdst[tt * 128:(tt + 1) * 128, :], OTILE, "dG")
                    P.wait("vector", "dG")
            if "dbgt" in dump:
                dbgC = dram("dbgtC", [128, 256], F32, "Internal")
                P.wait("gpsimd", "v")
                P.wait("gpsimd", "a")
                P.dma("gpsimd", dbgC, PREFTM, "dG")
            AR.freeb("wb", "qs0", "qs1", "otile", "wstg0", "wstg1")

        if upto >= 5:
            full_barrier()
            AR.freeb("xbt")
            WOB = bf16v("wob", 128, NI * 2048)

            def wob_c(c):
                return WOB[:, c * 2048:(c + 1) * 2048]
            WSTG3 = [f32v("wstg3a", 128, 2048), f32v("wstg3b", 128, 2048)]
            wstg3_buf = [Buf(P, "sync") for _ in range(2)]
            for c in range(NI):
                sb_ = wstg3_buf[c % 2]
                sb_.acquire()
                P.dma("sync", WSTG3[c % 2], Wo[c * 128:(c + 1) * 128, :], "dS")
                P.wait("vector", "dS")
                P.run("vector", lambda e, c=c: e.scalar_tensor_tensor(
                    out=wob_c(c), in0=WSTG3[c % 2], scalar=NW[:, c:c + 1],
                    in1=WSTG3[c % 2], op0=ALU.mult, op1=ALU.bypass),
                    "v" if True else None)
                sb_.consumed("v")
            AR.freeb("wstg3a", "wstg3b")

            if "dbgo" in dump:
                dbgo_d = dram("dbgo", [128, 4 * 8192], BF16, "Internal")
            if "dbgp" in dump:
                dbgp_d = dram("dbgp", [T, D], F32, "Internal")
                dbgs_d = dram("dbgs", [128, 16], F32, "Internal")
            QTS = bf16v("qts", 128, H * SUP)
            KTS = bf16v("kts", 128, H * SUP)
            qts_buf = Buf(P, "sync")
            kts_buf = Buf(P, "sync")
            KDS = bf16v("kds", 64, 2 * 2048)
            VS = bf16v("vs", 64, 2 * 2048)
            kds_buf = Buf(P, "gpsimd")
            vs_buf = Buf(P, "gpsimd")
            OSUP = bf16v("osup", 128, H * SUP)
            osup_buf = Buf(P, "scalar")
            STB = [bf16v("stb0", 64, 64), bf16v("stb1", 64, 64)]
            stb_buf = [Buf(P, "vector") for _ in range(2)]
            XF = f32v("xf", 128, 2048)
            xf_buf = Buf(P, "sync")
            X1PRE = f32v("x1pre", 128, 2048)
            x1pre_buf = Buf(P, "vector")
            X1 = f32v("x1", 128, 2048)
            x1_buf = Buf(P, "vector")
            XQ = bf16v("xq", 128, 2048)
            SRMS = f32v("srms", 128, 4)
            OSQ = [f32v("osq0", 128, 128), f32v("osq1", 128, 128)]
            osq_buf = [Buf(P, "scalar") for _ in range(2)]
            P.run("scalar", lambda e: e.preload_act_table(AF.Sqrt))

            sk_slab = sk.rearrange("(tl c p) d -> tl p c d", c=2, p=64)
            sv_slab = sv.rearrange("(tl c p) d -> tl p c d", c=2, p=64)

            mixidx = 0
            for g in range(NSUP):
                scols = slice(g * SUP, (g + 1) * SUP)
                qts_buf.acquire()
                for c in range(NI):
                    P.dma("sync", QTS[:, c * SUP:(c + 1) * SUP],
                          sq[scols, c * 128:(c + 1) * 128], "dS", transpose=True)
                kts_buf.acquire()
                for c in range(NI):
                    P.dma("sync", KTS[:, c * SUP:(c + 1) * SUP],
                          sk[scols, c * 128:(c + 1) * 128], "dS", transpose=True)
                P.wait("tensor", "dS")
                for lt in range(4):
                    tt = g * 4 + lt
                    kds_buf.acquire()
                    P.dma("gpsimd", KDS.rearrange("p (c d) -> p c d", c=2),
                          sk_slab[tt], "dG")
                    vs_buf.acquire()
                    P.dma("gpsimd", VS.rearrange("p (c d) -> p c d", c=2),
                          sv_slab[tt], "dG")
                    P.wait("tensor", "dG")
                    for cl in range(2):
                        for h in range(H):
                            par = mixidx % 2
                            colA = tt * 32 + cl * 16 + h
                            tok0 = lt * 128 + cl * 64
                            bS = banks[par * 3]
                            bO = banks[par * 3 + 1]
                            bU = banks[par * 3 + 2]
                            pS = psum[par * 3][0:64, 0:64]
                            pO = psum[par * 3 + 1][:, 0:64]
                            pU = psum[par * 3 + 2][:, 0:128]
                            bS.acquire()
                            P.run("tensor", lambda e, h=h, tok0=tok0, pS=pS: e.matmul(
                                pS, KTS[:, h * SUP + tok0:h * SUP + tok0 + 64],
                                QTS[:, h * SUP + tok0:h * SUP + tok0 + 64],
                                start=True, stop=True), "t")
                            sb = stb_buf[par]
                            sb.acquire()
                            P.wait("vector", "t")
                            P.run("vector", lambda e, pS=pS, colA=colA, par=par:
                                  e.scalar_tensor_tensor(
                                      out=STB[par], in0=pS,
                                      scalar=RPRODTM[0:64, colA:colA + 1],
                                      in1=MASKT, op0=ALU.mult, op1=ALU.mult), "v")
                            bS.consumed("v")
                            bO.acquire()
                            P.wait("tensor", "v")
                            P.wait("tensor", "a")
                            P.run("tensor", lambda e, cl=cl, h=h, par=par, pO=pO:
                                  e.matmul(pO,
                                           VS[:, cl * 2048 + h * 128:
                                              cl * 2048 + (h + 1) * 128],
                                           STB[par], start=True, stop=False))
                            P.run("tensor", lambda e, h=h, tok0=tok0, pO=pO: e.matmul(
                                pO, SBF[:, h * 128:(h + 1) * 128],
                                QTS[:, h * SUP + tok0:h * SUP + tok0 + 64],
                                start=False, stop=True), "t")
                            sb.consumed("t")
                            P.wait("scalar", "t")
                            osup_buf.acquire()
                            P.run("scalar", lambda e, pO=pO, h=h, tok0=tok0: e.copy(
                                out=OSUP[:, h * SUP + tok0:h * SUP + tok0 + 64],
                                in_=pO), "a")
                            bO.consumed("a")
                            bU.acquire()
                            P.run("tensor", lambda e, cl=cl, h=h, pU=pU: e.matmul(
                                pU, KDS[:, cl * 2048 + h * 128:cl * 2048 + (h + 1) * 128],
                                VS[:, cl * 2048 + h * 128:cl * 2048 + (h + 1) * 128],
                                start=True, stop=True), "t")
                            kds_buf.consumed("t")
                            vs_buf.consumed("t")
                            P.wait("vector", "t")
                            P.wait("vector", "a")
                            P.run("vector", lambda e, pU=pU, h=h, colA=colA:
                                  e.scalar_tensor_tensor(
                                      out=SF32[:, h * 128:(h + 1) * 128],
                                      in0=SF32[:, h * 128:(h + 1) * 128],
                                      scalar=ETOTTM[:, colA:colA + 1],
                                      in1=pU, op0=ALU.mult, op1=ALU.add), "v")
                            bU.consumed("v")
                            P.wait("scalar", "v")
                            P.run("scalar", lambda e, h=h: e.copy(
                                out=SBF[:, h * 128:(h + 1) * 128],
                                in_=SF32[:, h * 128:(h + 1) * 128]), "a")
                            mixidx += 1
                    bank = banks[6]
                    bank.acquire()
                    pR = psum[6][:, 0:1]
                    P.wait("scalar", "a")
                    for h in range(H):
                        ob = osq_buf[h % 2]
                        ob.acquire()
                        P.run("scalar", lambda e, h=h, lt=lt: e.square(
                            out=OSQ[h % 2],
                            in_=OSUP[:, h * SUP + lt * 128:h * SUP + (lt + 1) * 128]), "a")
                        osup_buf.consumed("a")
                        P.wait("tensor", "a")
                        P.run("tensor", lambda e, h=h, pR=pR: e.matmul(
                            pR, OSQ[h % 2], ONESF[:, 0:1],
                            start=(h == 0), stop=(h == H - 1)), "t")
                        ob.consumed("t")
                    P.wait("scalar", "t")
                    P.run("scalar", lambda e, pR=pR, lt=lt: e.activation(
                        out=SMALL[:, 8:9], in_=pR, func=AF.Sqrt,
                        scale=1.0 / D, bias=EPSRMSC), "a")
                    bank.consumed("a")
                    P.wait("vector", "a")
                    P.run("vector", lambda e, lt=lt: e.reciprocal(
                        out=SRMS[:, lt:lt + 1], in_=SMALL[:, 8:9]), "v")
                    P.wait("vector", "v")
                    xf_buf.acquire()
                    P.dma("sync", XF, x_in[tt * 128:(tt + 1) * 128, :], "dS")
                    P.wait("tensor", "dS")
                    x1pre_buf.acquire()
                    for sp in range(4):
                        bankw = banks[6 + sp % 2]
                        bankw.acquire()
                        pw = psum[6 + sp % 2][:, 0:512]
                        for hh in range(H):
                            P.run("tensor", lambda e, hh=hh, lt=lt, sp=sp, pw=pw,
                                  s=(hh == 0), st=(hh == H - 1): e.matmul(
                                pw, OSUP[:, hh * SUP + lt * 128:hh * SUP + (lt + 1) * 128],
                                wob_c(hh)[:, sp * 512:(sp + 1) * 512],
                                start=s, stop=st), "t" if hh == H - 1 else None)
                        osup_buf.consumed("t")
                        P.wait("vector", "t")
                        P.wait("vector", "a")
                        P.run("vector", lambda e, pw=pw, sp=sp, lt=lt:
                              e.scalar_tensor_tensor(
                                  out=X1PRE[:, sp * 512:(sp + 1) * 512], in0=pw,
                                  scalar=SRMS[:, lt:lt + 1],
                                  in1=XF[:, sp * 512:(sp + 1) * 512],
                                  op0=ALU.mult, op1=ALU.add),
                              "v" if sp == 3 else None)
                        bankw.consumed("v")
                    xf_buf.consumed("v")
                    if "dbgp" in dump:
                        P.wait("gpsimd", "v")
                        P.dma("gpsimd", dbgp_d[tt * 128:(tt + 1) * 128, :], X1PRE, "dG")
                        P.dma("gpsimd", dbgs_d[0:128, tt:tt + 1], SRMS[:, lt:lt + 1], "dG", slow=True)
                        P.wait("vector", "dG")
                        P.wait("scalar", "dG")
                    P.run("vector", lambda e: e.reduce_sum(
                        out=SMALL[:, 0:1], in_=X1PRE, axis=AX.X), "v")
                    P.wait("vector", "v")
                    P.run("vector", lambda e: e.tensor_scalar(
                        out=SMALL[:, 2:3], in0=SMALL[:, 0:1], scalar1=1.0 / D,
                        scalar2=None, op0=ALU.mult), "v")
                    P.wait("vector", "v")
                    x1_buf.acquire()
                    P.run("vector", lambda e: e.tensor_scalar(
                        out=X1, in0=X1PRE, scalar1=SMALL[:, 2:3], scalar2=None,
                        op0=ALU.subtract), "v")
                    P.wait("scalar", "v")
                    P.run("scalar", lambda e: e.activation(
                        out=X1PRE, in_=X1, func=AF.Square,
                        accum_out=SMALL[:, 1:2]), "a")
                    x1pre_buf.consumed("a")
                    P.wait("scalar", "a")
                    P.run("scalar", lambda e: e.activation(
                        out=SMALL[:, 4:5], in_=SMALL[:, 1:2], func=AF.Sqrt,
                        scale=1.0 / D, bias=EPSLNC), "a")
                    P.wait("vector", "a")
                    P.run("vector", lambda e: e.reciprocal(
                        out=SMALL[:, 5:6], in_=SMALL[:, 4:5]), "v")
                    P.wait("vector", "v")
                    P.run("vector", lambda e: e.tensor_scalar(
                        out=X1, in0=X1, scalar1=SMALL[:, 5:6], scalar2=None,
                        op0=ALU.mult), "v")
                    P.wait("gpsimd", "v")
                    P.dma("gpsimd", sx1[tt * 128:(tt + 1) * 128, :], X1, "dG")
                    x1_buf.consumed("dG")
                    P.run("vector", lambda e: e.reduce_max(
                        out=SMALL[:, 6:7], in_=X1, axis=AX.X,
                        apply_absolute_value=True), "v")
                    P.wait("vector", "v")
                    P.run("vector", lambda e, tt=tt: e.tensor_scalar(
                        out=S1COL[:, tt:tt + 1], in0=SMALL[:, 6:7], scalar1=1.0 / 127.0,
                        scalar2=EPS_Q, op0=ALU.mult, op1=ALU.add), "v")
                    P.wait("vector", "v")
                    P.run("vector", lambda e, tt=tt: e.reciprocal(
                        out=SMALL[:, 7:8], in_=S1COL[:, tt:tt + 1]), "v")
                    P.wait("scalar", "v")
                    P.run("scalar", lambda e: e.activation(
                        out=X1PRE, in_=X1, func=AF.Identity, scale=SMALL[:, 7:8],
                        bias=MAGICC), "a")
                    x1_buf.consumed("a")
                    x1pre_buf.consumed("a")
                    P.wait("vector", "a")
                    P.run("vector", lambda e: e.tensor_scalar(
                        out=XQ, in0=X1PRE, scalar1=MAGIC, scalar2=None,
                        op0=ALU.subtract), "v")
                    x1pre_buf.consumed("v")
                    P.wait("gpsimd", "v")
                    P.dma("gpsimd", sxq[tt * 128:(tt + 1) * 128, :], XQ, "dG")
                    P.wait("vector", "dG")
                if "dbgo" in dump:
                    P.wait("gpsimd", "a")
                    P.dma("gpsimd", dbgo_d[0:128, g * 8192:(g + 1) * 8192], OSUP, "dG")
                    P.wait("vector", "dG")
                    P.wait("scalar", "dG")
                qts_buf.consumed("t")
                kts_buf.consumed("t")
            AR.freeb("wob", "qts", "kts", "kds", "vs", "osup", "stb0", "stb1",
                     "xf", "x1pre", "x1", "xq", "osq0", "osq1", "srms")

        if upto >= 6:
            full_barrier()
            AR.freeb("preftm", "ksctm", "etottm", "rprodtm", "sf32", "sbf",
                     "maskt", "sela", "selb")
            P.wait("vector", "v")
            for tt in range(NT):
                P.run("vector", lambda e, tt=tt: e.tensor_scalar(
                    out=S1WG[:, tt:tt + 1], in0=S1COL[:, tt:tt + 1],
                    scalar1=WSCOL[:, 0:1], scalar2=None, op0=ALU.mult))
                P.run("vector", lambda e, tt=tt: e.tensor_scalar(
                    out=S1WU[:, tt:tt + 1], in0=S1COL[:, tt:tt + 1],
                    scalar1=WSCOL[:, 1:2], scalar2=None, op0=ALU.mult), "v")
            P.wait("vector", "v")
            P.wait("scalar", "v")

            XQT = bf16v("xqt", 128, NI * 2048)

            def xqt_c(c):
                return XQT[:, c * 2048:(c + 1) * 2048]
            P.wait("sync", "dG")
            for c in range(NI):
                P.dma("sync", xqt_c(c), sxq[0:T, c * 128:(c + 1) * 128], "dS",
                      transpose=True)

            WGS = [bf16v(f"wgs{i}", 128, NI * 512) for i in range(2)]
            WUS = [bf16v(f"wus{i}", 128, NI * 512) for i in range(2)]
            wslab2_buf = [Buf(P, "sync") for _ in range(2)]
            HPARTS = f32v("hparts", 128, 16 * NT)
            GT = [f32v(f"gt{i}", 128, 512) for i in range(2)]
            gt_buf = [Buf(P, "scalar") for _ in range(2)]
            HT = [f32v(f"ht{i}", 128, 512) for i in range(2)]
            ht_buf = [Buf(P, "vector") for _ in range(2)]
            P.run("scalar", lambda e: e.preload_act_table(AF.Silu))

            for sp in range(FF // 512):
                b = sp % 2
                wslab2_buf[b].acquire()
                for c in range(NI):
                    P.dma("sync", WGS[b][:, c * 512:(c + 1) * 512],
                          swg[sp * 512:(sp + 1) * 512, c * 128:(c + 1) * 128],
                          "dS", transpose=True)
                    P.dma("sync", WUS[b][:, c * 512:(c + 1) * 512],
                          swu[sp * 512:(sp + 1) * 512, c * 128:(c + 1) * 128],
                          "dS", transpose=True)
                P.wait("tensor", "dS")
                for tt in range(NT):
                    tcols = slice(tt * 128, (tt + 1) * 128)
                    par = tt % 2
                    bG, bU = banks[par * 2], banks[par * 2 + 1]
                    pG = psum[par * 2][:, 0:512]
                    pU = psum[par * 2 + 1][:, 0:512]
                    bG.acquire()
                    bU.acquire()
                    for c in range(NI):
                        P.run("tensor", lambda e, c=c, pG=pG, tcols=tcols, b=b,
                              s=(c == 0), st=(c == NI - 1): e.matmul(
                            pG, xqt_c(c)[:, tcols],
                            WGS[b][:, c * 512:(c + 1) * 512], start=s, stop=st))
                    for c in range(NI):
                        P.run("tensor", lambda e, c=c, pU=pU, tcols=tcols, b=b,
                              s=(c == 0), st=(c == NI - 1): e.matmul(
                            pU, xqt_c(c)[:, tcols],
                            WUS[b][:, c * 512:(c + 1) * 512], start=s, stop=st),
                            "t" if c == NI - 1 else None)
                    wslab2_buf[b].consumed("t")
                    gb = gt_buf[par]
                    gb.acquire()
                    P.wait("scalar", "t")
                    P.run("scalar", lambda e, pG=pG, par=par, tt=tt: e.activation(
                        out=GT[par], in_=pG, func=AF.Silu,
                        scale=S1WG[:, tt:tt + 1]), "a")
                    bG.consumed("a")
                    hb = ht_buf[par]
                    hb.acquire()
                    P.wait("vector", "a")
                    P.run("vector", lambda e, pU=pU, par=par, tt=tt:
                          e.scalar_tensor_tensor(
                              out=HT[par], in0=pU, scalar=S1WU[:, tt:tt + 1],
                              in1=GT[par], op0=ALU.mult, op1=ALU.mult), "v")
                    bU.consumed("v")
                    gb.consumed("v")
                    P.run("vector", lambda e, par=par, tt=tt, sp=sp: e.reduce_max(
                        out=HPARTS[:, tt * 16 + sp:tt * 16 + sp + 1], in_=HT[par],
                        axis=AX.X, apply_absolute_value=True), "v")
                    P.wait("scalar", "v")
                    P.dma("scalar", sh[tt * 128:(tt + 1) * 128,
                                       sp * 512:(sp + 1) * 512], HT[par], "dA")
                    hb.consumed("dA")
            P.wait("vector", "v")
            for tt in range(NT):
                P.run("vector", lambda e, tt=tt: e.reduce_max(
                    out=HMAXTM[:, tt:tt + 1], in_=HPARTS[:, tt * 16:(tt + 1) * 16],
                    axis=AX.X), "v" if tt == NT - 1 else None)
            P.wait("vector", "v")
            for tt in range(NT):
                P.run("vector", lambda e, tt=tt: e.tensor_scalar(
                    out=RS2TM[:, tt:tt + 1], in0=HMAXTM[:, tt:tt + 1],
                    scalar1=1.0 / 127.0, scalar2=EPS_Q, op0=ALU.mult, op1=ALU.add),
                    "v" if tt == NT - 1 else None)
            P.wait("vector", "v")
            for tt in range(NT):
                P.run("vector", lambda e, tt=tt: e.tensor_scalar(
                    out=S2WD[:, tt:tt + 1], in0=RS2TM[:, tt:tt + 1],
                    scalar1=WSCOL[:, 2:3], scalar2=None, op0=ALU.mult),
                    "v" if tt == NT - 1 else None)
            P.wait("vector", "v")
            P.run("vector", lambda e: e.reciprocal(out=RS2TM, in_=RS2TM), "v")
            P.wait("vector", "v")
            P.wait("scalar", "v")
            AR.freeb("xqt", "wgs0", "wgs1", "wus0", "wus1", "gt0", "gt1",
                     "ht0", "ht1", "hparts")

        if upto >= 7:
            full_barrier()
            P.run("scalar", lambda e: e.preload_act_table(AF.Sqrt))
            HROW = f32v("hrow", 128, FF)
            hrow_buf = Buf(P, "sync")
            HQ = bf16v("hq", 128, FF)
            P.wait("sync", "dA")
            for tt in range(NT):
                rows = slice(tt * 128, (tt + 1) * 128)
                hrow_buf.acquire()
                P.dma("sync", HROW, sh[rows, :], "dS")
                P.wait("scalar", "dS")
                P.run("scalar", lambda e, tt=tt: e.activation(
                    out=HROW, in_=HROW, func=AF.Identity, scale=RS2TM[:, tt:tt + 1],
                    bias=MAGICC), "a")
                P.wait("vector", "a")
                P.run("vector", lambda e: e.tensor_scalar(
                    out=HQ, in0=HROW, scalar1=MAGIC, scalar2=None,
                    op0=ALU.subtract), "v")
                hrow_buf.consumed("v")
                P.wait("gpsimd", "v")
                P.dma("gpsimd", shq[rows, :], HQ, "dG")
                P.wait("vector", "dG")
            AR.freeb("hrow", "hq")
            full_barrier()

            HQT = bf16v("hqt", 128, NO * 512)

            def hqt_c(c):
                return HQT[:, c * 512:(c + 1) * 512]
            WDS = bf16v("wds", 128, NO * 512)

            def wds_c(c):
                return WDS[:, c * 512:(c + 1) * 512]
            hqt_buf = Buf(P, "sync")
            wds_buf = Buf(P, "sync")
            X2 = [f32v(f"x2_{i}", 128, 2048) for i in range(4)]
            x2_buf = [Buf(P, "vector") for _ in range(4)]
            X1T = f32v("x1t", 128, 2048)
            x1t_buf = Buf(P, "sync")
            OUTT = X1T
            outt_buf = x1t_buf

            P.wait("sync", "dG")
            for g in range(NSUP):
                hqt_buf.acquire()
                for c in range(NO):
                    P.dma("sync", hqt_c(c),
                          shq[g * SUP:(g + 1) * SUP, c * 128:(c + 1) * 128],
                          "dS", transpose=True)
                for lt in range(4):
                    x2_buf[lt].acquire()
                for ds in range(4):
                    wds_buf.acquire()
                    for c in range(NO):
                        P.dma("sync", wds_c(c),
                              swd[ds * 512:(ds + 1) * 512, c * 128:(c + 1) * 128],
                              "dS", transpose=True)
                    P.wait("tensor", "dS")
                    for lt in range(4):
                        tt = g * 4 + lt
                        bank = banks[lt]
                        bank.acquire()
                        pD = psum[lt][:, 0:512]
                        for c in range(NO):
                            P.run("tensor", lambda e, c=c, pD=pD, lt=lt,
                                  s=(c == 0), st=(c == NO - 1): e.matmul(
                                pD, hqt_c(c)[:, lt * 128:(lt + 1) * 128],
                                wds_c(c), start=s, stop=st),
                                "t" if c == NO - 1 else None)
                        wds_buf.consumed("t")
                        if ds == 3:
                            hqt_buf.consumed("t")
                        P.wait("vector", "t")
                        P.run("vector", lambda e, pD=pD, ds=ds, lt=lt, tt=tt:
                              e.tensor_scalar(
                                  out=X2[lt][:, ds * 512:(ds + 1) * 512], in0=pD,
                                  scalar1=S2WD[:, tt:tt + 1], scalar2=None,
                                  op0=ALU.mult), "v")
                        bank.consumed("v")
                for lt in range(4):
                    tt = g * 4 + lt
                    rows = slice(tt * 128, (tt + 1) * 128)
                    x1t_buf.acquire()
                    P.dma("sync", X1T, sx1[rows, :], "dS")
                    P.wait("vector", "dS")
                    P.run("vector", lambda e, lt=lt: e.tensor_add(
                        out=X2[lt], in0=X2[lt], in1=X1T), "v")
                    x1t_buf.consumed("v")
                    P.run("vector", lambda e, lt=lt: e.reduce_sum(
                        out=SMALL[:, 0:1], in_=X2[lt], axis=AX.X), "v")
                    P.wait("vector", "v")
                    P.run("vector", lambda e: e.tensor_scalar(
                        out=SMALL[:, 2:3], in0=SMALL[:, 0:1], scalar1=1.0 / D,
                        scalar2=None, op0=ALU.mult), "v")
                    P.wait("vector", "v")
                    outt_buf.acquire()
                    P.run("vector", lambda e, lt=lt: e.tensor_scalar(
                        out=OUTT, in0=X2[lt], scalar1=SMALL[:, 2:3], scalar2=None,
                        op0=ALU.subtract), "v")
                    x2_buf[lt].consumed("v")
                    P.wait("scalar", "v")
                    P.run("scalar", lambda e, lt=lt: e.activation(
                        out=X2[lt], in_=OUTT, func=AF.Square,
                        accum_out=SMALL[:, 1:2]), "a")
                    x2_buf[lt].consumed("a")
                    P.wait("scalar", "a")
                    P.run("scalar", lambda e: e.activation(
                        out=SMALL[:, 4:5], in_=SMALL[:, 1:2], func=AF.Sqrt,
                        scale=1.0 / D, bias=EPSLNC), "a")
                    P.wait("vector", "a")
                    P.run("vector", lambda e: e.reciprocal(
                        out=SMALL[:, 5:6], in_=SMALL[:, 4:5]), "v")
                    P.wait("vector", "v")
                    P.run("vector", lambda e: e.tensor_scalar(
                        out=OUTT, in0=OUTT, scalar1=SMALL[:, 5:6], scalar2=None,
                        op0=ALU.mult), "v")
                    P.wait("gpsimd", "v")
                    P.dma("gpsimd", out_d[rows, :], OUTT, "dG")
                    outt_buf.consumed("dG")

        block = stack.enter_context(nc.Block())

        @block.sync
        def _(e):
            for fn in P.ops["sync"]:
                fn(e)

        @block.gpsimd
        def _(e):
            for fn in P.ops["gpsimd"]:
                fn(e)
            e.wait_ge(P.sems["dG"], P.cnt["dG"])
            e.wait_ge(P.sems["dA"], P.cnt["dA"])

        @block.vector
        def _(e):
            for fn in P.ops["vector"]:
                fn(e)

        @block.scalar
        def _(e):
            for fn in P.ops["scalar"]:
                fn(e)

        @block.tensor
        def _(e):
            for fn in P.ops["tensor"]:
                fn(e)

    return nc


def make_consts():
    maskt = np.zeros((64, 64), np.float32)
    for j in range(64):
        maskt[j, j:] = 1.0
    sela = np.zeros((128, 128), np.float32)
    sela[63, :] = 1.0
    selb = np.zeros((128, 128), np.float32)
    selb[127, :] = 1.0
    onesf = np.ones((128, 128), np.float32)
    identf = np.eye(128, dtype=np.float32)
    consts = np.zeros((128, 8), np.float32)
    consts[:, 0] = MAGIC
    consts[:, 1] = EPS_LN
    consts[:, 2] = EPS_RMS
    return dict(maskt=maskt, sela=sela, selb=selb, onesf=onesf, identf=identf,
                consts=consts)


def make_in_maps(L0_attn_x, L0_attn_Wq, L0_attn_Wk, L0_attn_Wv, L0_attn_Wf,
                 L0_attn_Wo, L0_attn_norm_w, L0_ffn_w_gate, L0_ffn_w_up,
                 L0_ffn_w_down, n_cores=8):
    consts = make_consts()
    common = dict(
        Wq=np.ascontiguousarray(L0_attn_Wq), Wk=np.ascontiguousarray(L0_attn_Wk),
        Wv=np.ascontiguousarray(L0_attn_Wv), Wf=np.ascontiguousarray(L0_attn_Wf),
        Wo=np.ascontiguousarray(L0_attn_Wo),
        norm_w=np.ascontiguousarray(L0_attn_norm_w),
        w_gate=np.ascontiguousarray(L0_ffn_w_gate),
        w_up=np.ascontiguousarray(L0_ffn_w_up),
        w_down=np.ascontiguousarray(L0_ffn_w_down), **consts)
    in_maps = []
    for b in range(n_cores):
        m = dict(common)
        m["x"] = np.ascontiguousarray(np.asarray(L0_attn_x)[b])
        in_maps.append(m)
    return in_maps


def kernel(L0_attn_x, L0_attn_Wq, L0_attn_Wk, L0_attn_Wv, L0_attn_Wf, L0_attn_Wo,
           L0_attn_norm_w, L0_ffn_w_gate, L0_ffn_w_up, L0_ffn_w_down):
    nc = build_nc()
    in_maps = make_in_maps(L0_attn_x, L0_attn_Wq, L0_attn_Wk, L0_attn_Wv,
                           L0_attn_Wf, L0_attn_Wo, L0_attn_norm_w,
                           L0_ffn_w_gate, L0_ffn_w_up, L0_ffn_w_down)
    res = run_bass_kernel_spmd(nc, in_maps, core_ids=list(range(8)))
    return np.stack([np.asarray(res.results[b]["out"], np.float32)
                     for b in range(8)])
